# version 01 (final)
# baseline (speedup 1.0000x reference)
import numpy as np
import ml_dtypes
import concourse.bass as bass
import concourse.mybir as mybir
from concourse.bass_utils import run_bass_kernel_spmd

F32 = mybir.dt.float32
BF16 = mybir.dt.bfloat16
AF = mybir.ActivationFunctionType
ALU = mybir.AluOpType
SP = mybir.EngineType.SP

D = 4096
DC = 32
ISQ_HD = float(1.0 / np.sqrt(128.0))
EPS = 1e-5
NBLK = 32768
RKB = 4194304
COMPUTE = ("tensor", "scalar", "vector")


def dap(t, off, dims):
    return bass.AP(tensor=t, offset=off, ap=[list(d) for d in dims])


class Prog:

    def __init__(self, nc):
        self.nc = nc
        self.ops = []
        self.cnt = {k: 0 for k in ("dg", "dh", "tt", "ss", "vv", "cc")}
        self.sems = {}
        self.guard = {}
        self.waited = {}

    def wait(self, eng, sem, val):
        if val is None or val <= 0:
            return
        assert val <= self.cnt[sem], (eng, sem, val, self.cnt[sem])
        if self.waited.get((eng, sem), 0) >= val:
            return
        self.waited[(eng, sem)] = val
        self.ops.append((eng, lambda e, s=sem, v=val: e.wait_ge(self.sems[s], v)))

    def waits(self, eng, pairs):
        for sem, val in pairs:
            self.wait(eng, sem, val)

    def gw(self, eng, slot):
        if slot is not None and slot in self.guard:
            for sem, val in self.guard[slot]:
                self.wait(eng, sem, val)

    def setg(self, slot, sem, val):
        self.guard[slot] = [(sem, val)]

    def seed(self, *slots):
        pairs = [(k, self.cnt[k]) for k in ("tt", "ss", "vv", "dh", "dg")
                 if self.cnt[k] > 0]
        for sl in slots:
            self.guard[sl] = list(pairs)

    def barrier(self):
        t, s, v = self.cnt["tt"], self.cnt["ss"], self.cnt["vv"]
        self.wait("tensor", "ss", s)
        self.wait("tensor", "vv", v)
        self.wait("scalar", "tt", t)
        self.wait("scalar", "vv", v)
        self.wait("vector", "tt", t)
        self.wait("vector", "ss", s)

    def dma(self, q, out, in_, slot=None, waits=()):
        eng = "gpsimd" if q == "dg" else "sync"
        self.gw(eng, slot)
        self.waits(eng, waits)
        self.cnt[q] += 16
        v = self.cnt[q]
        self.ops.append((eng, lambda e, o=out, i=in_, s=q:
                         e.dma_start(out=o, in_=i).then_inc(self.sems[s], 16)))
        return v

    def dma_dyn(self, q, outfn, infn, slot=None, waits=()):
        eng = "gpsimd" if q == "dg" else "sync"
        self.gw(eng, slot)
        self.waits(eng, waits)
        self.cnt[q] += 16
        v = self.cnt[q]
        self.ops.append((eng, lambda e, o=outfn, i=infn, s=q:
                         e.dma_start(out=o(), in_=i()).then_inc(self.sems[s], 16)))
        return v

    def op(self, eng, fn, sem=None, waits=(), slot=None):
        self.gw(eng, slot)
        self.waits(eng, waits)
        if sem is not None:
            self.cnt[sem] += 1
            v = self.cnt[sem]
            self.ops.append((eng, lambda e, f=fn, s=sem:
                             f(e).then_inc(self.sems[s], 1)))
            return v
        self.ops.append((eng, lambda e, f=fn: f(e)))
        return None

    def collective(self, ins, outs, waits=()):
        self.waits("gpsimd", waits)
        self.cnt["cc"] += 1
        v = self.cnt["cc"]
        self.ops.append(("gpsimd", lambda e, i=ins, o=outs: e.collective_compute(
            "AllGather", ALU.bypass,
            replica_groups=[[0, 1, 2, 3], [4, 5, 6, 7]],
            ins=[i], outs=[o]).then_inc(self.sems["cc"], 1)))
        return v


def build_program(tap=None):
    nc = bass.Bass("TRN2", target_bir_lowering=False)

    def par(name, shape):
        return nc.declare_dram_parameter(name, shape, F32, isOutput=False)

    xt = par("xt", [D, 1024])
    wt_w = par("wt_w", [D, D])
    wt_wo = par("wt_wo", [D, D])
    wt_q = par("wt_q", [D, 1024])
    wt_k = par("wt_k", [D, 1024])
    wt_v = par("wt_v", [D, 1024])
    wt_mq = par("wt_mq", [D, 256])
    wt_mk = par("wt_mk", [D, 256])
    wt_a = par("wt_a", [D, 128])
    bflat_in = par("bflat", [128, D])
    b_fm_in = par("b_fm", [128, 32])
    bo_fm_in = par("bo_fm", [128, 32])
    bq_fm_in = par("bq_fm", [128, 8])
    bk_fm_in = par("bk_fm", [128, 8])
    bv_fm_in = par("bv_fm", [128, 8])
    selq_in = par("selq", [256, 8])
    e16_in = par("e16", [8, 128])
    trimask_in = par("trimask", [128, 128])
    out_t = nc.declare_dram_parameter("out_t", [D, 1024], F32, isOutput=True)

    pay_n2 = nc.dram_tensor("pay_n2", [128, NBLK], BF16)
    gath_n2 = nc.dram_tensor("gath_n2", [512, NBLK], BF16)
    q_loc = nc.dram_tensor("q_loc", [128, NBLK], BF16)
    k_loc = nc.dram_tensor("k_loc", [128, NBLK], BF16)
    v_loc = nc.dram_tensor("v_loc", [128, NBLK], BF16)
    pay_ao = nc.dram_tensor("pay_ao", [128, NBLK], BF16)
    gath_ao = nc.dram_tensor("gath_ao", [512, NBLK], BF16)

    A = nc.alloc_sbuf_tensor
    R = A("R", [128, 32768], F32)
    S = A("Sb", [128, 32768], BF16)
    pb = [A(f"pb{i}", [128, 1024], BF16) for i in range(4)]
    pf0 = A("pf0", [128, 512], F32)
    pf1 = A("pf1", [128, 512], F32)
    statf = A("statf", [128, 512], F32)
    b_fm = A("b_fm_sb", [128, 32], F32)
    bo_fm = A("bo_fm_sb", [128, 32], F32)
    bq_fm = A("bq_fm_sb", [128, 8], F32)
    bk_fm = A("bk_fm_sb", [128, 8], F32)
    bv_fm = A("bv_fm_sb", [128, 8], F32)
    biascol = A("biascol", [128, 1], F32)
    selq_sb = A("selq_sb", [128, 16], BF16)
    e16_sb = A("e16_sb", [8, 128], BF16)
    trimask = A("trimask_sb", [128, 128], BF16)
    ones128b = A("ones128b", [128, 1], BF16)
    onesneg = A("onesneg", [1, 128], F32)
    assert nc.sbuf_bytes_remaining >= 0, nc.sbuf_bytes_remaining

    ps = [nc.alloc_psum_tensor(f"ps{i}", [128, 512], F32) for i in range(8)]

    P = Prog(nc)
    st_mean = statf[0:1, :]
    st_msq = pf1[0:1, :]
    st_chain = pf0[0:1, :]
    w_sb = pf1[0:8, :].bitcast(BF16)
    we_sb = pf0[0:8, :].bitcast(BF16)

    c_dh = 0
    for (dst, src) in ((b_fm, b_fm_in), (bo_fm, bo_fm_in), (bq_fm, bq_fm_in),
                       (bk_fm, bk_fm_in), (bv_fm, bv_fm_in)):
        c_dh = P.dma("dh", dst[:, :], src[:, :])
    P.dma("dg", selq_sb[:, 0:8], selq_in[0:128, :])
    P.dma("dg", selq_sb[:, 8:16], selq_in[128:256, :])
    P.dma("dg", e16_sb[:, :], e16_in[:, :])
    c_dg_const = P.dma("dg", trimask[:, :], trimask_in[:, :])
    P.op("vector", lambda e: e.memset(ones128b[:, :], 1.0))
    P.op("vector", lambda e: e.memset(onesneg[0:1, 0:128], 1.0), sem="vv")
    s_halfb = P.op("scalar", lambda e: e.mul(out=b_fm[:, :], in_=b_fm[:, :],
                                             mul=0.5),
                   sem="ss", waits=[("dh", c_dh)])
    x_dh = [P.dma("dh", R[:, dc * 1024:(dc + 1) * 1024],
                  xt[dc * 128:(dc + 1) * 128, :]) for dc in range(DC)]

    def emit_ln(dst, xwaits=None):
        P.barrier()
        P.seed("ln_xb", "ln_n2st0", "ln_n2st1")
        for tp in range(2):
            for dc in range(DC):
                col = dc * 1024 + tp * 512
                w = [("dh", xwaits[dc])] if xwaits else ()
                v1 = P.op("vector", lambda e, c=col: e.tensor_copy(
                    out=pb[2][:, 0:512], in_=R[:, c:c + 512]),
                    sem="vv", slot="ln_xb", waits=w)
                v2 = P.op("vector", lambda e: e.tensor_mul(
                    out=pb[2][:, 512:1024], in0=pb[2][:, 0:512],
                    in1=pb[2][:, 0:512]), sem="vv")
                P.op("tensor", lambda e, f=(dc == 0), l=(dc == DC - 1):
                     e.matmul(ps[0][0:1, :], ones128b[:, :], pb[2][:, 0:512],
                              start=f, stop=l), waits=[("vv", v1)])
                t2 = P.op("tensor", lambda e, f=(dc == 0), l=(dc == DC - 1):
                          e.matmul(ps[1][0:1, :], ones128b[:, :],
                                   pb[2][:, 512:1024], start=f, stop=l),
                          sem="tt", waits=[("vv", v2)])
                P.setg("ln_xb", "tt", t2)
            t_stats = P.cnt["tt"]
            s1 = P.op("scalar", lambda e: e.mul(out=st_mean, in_=ps[0][0:1, :],
                                                mul=1.0 / D),
                      sem="ss", waits=[("tt", t_stats)])
            v3 = P.op("vector", lambda e: e.tensor_mul(out=st_msq, in0=st_mean,
                                                       in1=st_mean),
                      sem="vv", waits=[("ss", s1)])
            v3 = P.op("vector", lambda e: e.tensor_scalar_sub(
                out=st_msq, in0=st_msq, scalar1=EPS), sem="vv",
                waits=[("vv", v3)])
            v4 = P.op("vector", lambda e: e.scalar_tensor_tensor(
                out=st_chain, in0=ps[1][0:1, :], scalar=1.0 / D, in1=st_msq,
                op0=ALU.mult, op1=ALU.subtract), sem="vv", waits=[("vv", v3)])
            s2 = P.op("scalar", lambda e: e.activation(out=st_chain,
                                                       in_=st_chain,
                                                       func=AF.Ln),
                      sem="ss", waits=[("vv", v4)])
            s3 = P.op("scalar", lambda e: e.activation(out=st_chain,
                                                       in_=st_chain,
                                                       func=AF.Exp, scale=-0.5),
                      sem="ss", waits=[("ss", s2)])
            P.op("tensor", lambda e: e.matmul(
                ps[2][:, :], onesneg[0:1, 0:128], st_mean, start=True,
                stop=True), waits=[("vv", v4)])
            t4 = P.op("tensor", lambda e: e.matmul(
                ps[3][:, :], onesneg[0:1, 0:128], st_chain, start=True,
                stop=True), sem="tt", waits=[("ss", s3)])
            for dc in range(DC):
                col = dc * 1024 + tp * 512
                P.op("vector", lambda e, c=col: e.tensor_sub(
                    out=pf0[:, :], in0=R[:, c:c + 512], in1=ps[2][:, :]),
                    waits=[("tt", t4)] if dc == 0 else ())
                if dst == "S":
                    P.op("vector", lambda e, c=col: e.tensor_mul(
                        out=S[:, c:c + 512], in0=pf0[:, :], in1=ps[3][:, :]),
                        sem="vv")
                else:
                    pr = dc % 2
                    v6 = P.op("vector", lambda e, p=pr: e.tensor_mul(
                        out=pb[3][:, p * 512:(p + 1) * 512], in0=pf0[:, :],
                        in1=ps[3][:, :]), sem="vv", slot=f"ln_n2st{pr}")
                    off = (dc * 128) * 1024 + tp * 512
                    dv = P.dma("dh", dap(pay_n2, off, [[1024, 128], [1, 512]]),
                               pb[3][:, pr * 512:(pr + 1) * 512],
                               waits=[("vv", v6)])
                    P.setg(f"ln_n2st{pr}", "dh", dv)

    def emit_ffn():
        P.barrier()
        P.seed("mw0", "mw1", "mw2", "mw3", "wm0", "wm1", "wm2", "bfslab")
        n_ready = P.cnt["vv"]
        for iw, wsrc in ((0, wt_mq), (1, wt_mk)):
            for dc in range(DC):
                sl = dc % 4
                mw = pb[0][:, sl * 256:(sl + 1) * 256]
                dv = P.dma("dg", mw, wsrc[dc * 128:(dc + 1) * 128, :],
                           slot=f"mw{sl}")
                tl = None
                for ec in range(2):
                    for tp in range(2):
                        wts = []
                        if ec == 0 and tp == 0:
                            wts.append(("dg", dv))
                            if dc == 0 and iw == 0:
                                wts.append(("dg", c_dg_const))
                            if dc == 0 and iw == 1:
                                wts += P.guard["gate_ps"]
                        tl = P.op("tensor", lambda e, m=mw, b_=ec * 2 + tp,
                                  E=ec, c=dc * 1024 + tp * 512, f=(dc == 0),
                                  l=(dc == DC - 1): e.matmul(
                                      ps[b_][:, :], m[:, E * 128:(E + 1) * 128],
                                      S[:, c:c + 512], start=f, stop=l),
                                  waits=wts,
                                  sem="tt" if (ec == 1 and tp == 1) else None)
                P.setg(f"mw{sl}", "tt", tl)
            t_proj = P.cnt["tt"]
            for ec in range(2):
                for tp in range(2):
                    if iw == 0:
                        P.op("scalar", lambda e, b_=ec * 2 + tp, E=ec, T=tp:
                             e.copy(out=pb[2 + E][:, T * 512:(T + 1) * 512],
                                    in_=ps[b_][:, :]),
                             sem="ss", waits=[("tt", t_proj)])
                    else:
                        P.op("vector", lambda e, b_=ec * 2 + tp, E=ec, T=tp:
                             e.tensor_tensor(
                                 out=pb[2 + E][:, T * 512:(T + 1) * 512],
                                 in0=pb[2 + E][:, T * 512:(T + 1) * 512],
                                 in1=ps[b_][:, :], op=ALU.mult),
                             sem="vv",
                             waits=[("tt", t_proj)] +
                             ([("ss", P.cnt["ss"])] if (ec == 0 and tp == 0)
                              else []))
            if iw == 0:
                s_q = P.cnt["ss"]
                P.guard["gate_ps"] = [("ss", s_q)]
        v_prod = P.cnt["vv"]
        P.guard["gate_ps"] = [("vv", v_prod)]
        for tp in range(2):
            P.op("tensor", lambda e, c=tp * 512: e.matmul(
                ps[4][0:8, :], selq_sb[:, 0:8], pb[2][:, c:c + 512],
                start=True, stop=False),
                waits=[("vv", v_prod)] if tp == 0 else (),
                slot="sc_ps")
            t_sc = P.op("tensor", lambda e, c=tp * 512: e.matmul(
                ps[4][0:8, :], selq_sb[:, 8:16], pb[3][:, c:c + 512],
                start=False, stop=True), sem="tt")
            s_we = P.op("scalar", lambda e, c=tp * 512: e.activation(
                out=we_sb[0:8, c:c + 512], in_=ps[4][0:8, :], func=AF.Exp),
                sem="ss", waits=[("tt", t_sc)])
            P.setg("sc_ps", "ss", s_we)
            t_den = P.op("tensor", lambda e, c=tp * 512: e.matmul(
                ps[5][0:1, :], ones128b[0:8, :], we_sb[0:8, c:c + 512],
                start=True, stop=True), sem="tt", waits=[("ss", s_we)],
                slot="den_ps")
            v_rc = P.op("vector", lambda e: e.reciprocal(
                out=st_mean, in_=ps[5][0:1, :]), sem="vv",
                waits=[("tt", t_den)])
            P.setg("den_ps", "vv", v_rc)
            t_wb = P.op("tensor", lambda e: e.matmul(
                ps[6][0:8, :], onesneg[0:1, 0:8], st_mean, start=True,
                stop=True), sem="tt", waits=[("vv", v_rc)], slot="wb_ps")
            v_w = P.op("vector", lambda e, c=tp * 512: e.tensor_tensor(
                out=w_sb[0:8, c:c + 512], in0=we_sb[0:8, c:c + 512],
                in1=ps[6][0:8, :], op=ALU.mult), sem="vv",
                waits=[("tt", t_wb)])
            P.setg("wb_ps", "vv", v_w)
        v_w_done = P.cnt["vv"]
        t_sc_done = P.cnt["tt"]
        for dc in range(DC):
            sl = dc % 4
            aw = pb[0][:, sl * 256:sl * 256 + 128]
            dv = P.dma("dg", aw, wt_a[dc * 128:(dc + 1) * 128, :],
                       slot=f"mw{sl}")
            tl = None
            for tp in range(2):
                wts = [("dg", dv)] if tp == 0 else []
                if dc == 0 and tp == 0:
                    wts += P.guard["gate_ps"]
                tl = P.op("tensor", lambda e, m=aw, b_=tp,
                          c=dc * 1024 + tp * 512, f=(dc == 0),
                          l=(dc == DC - 1): e.matmul(
                              ps[b_][:, :], m[:, 0:128], S[:, c:c + 512],
                              start=f, stop=l),
                          waits=wts, sem="tt" if tp == 1 else None)
            P.setg(f"mw{sl}", "tt", tl)
        t_gate = P.cnt["tt"]
        for tp in range(2):
            P.op("scalar", lambda e, b_=tp, T=tp: e.copy(
                out=pb[2][:, T * 512:(T + 1) * 512], in_=ps[b_][:, :]),
                sem="ss", waits=[("tt", t_gate)])
        s_h = P.cnt["ss"]
        for tp in range(2):
            t_we2 = P.op("tensor", lambda e, c=tp * 512: e.matmul(
                ps[2][:, :], e16_sb[:, :], w_sb[0:8, c:c + 512], start=True,
                stop=True), sem="tt",
                waits=[("vv", v_w_done)], slot="g_ps")
            v_g = P.op("vector", lambda e, c=tp * 512: e.tensor_tensor(
                out=pb[3][:, c:c + 512], in0=pb[2][:, c:c + 512],
                in1=ps[2][:, :], op=ALU.mult), sem="vv",
                waits=[("tt", t_we2), ("ss", s_h)])
            P.setg("g_ps", "vv", v_g)
        v_g_done = P.cnt["vv"]
        evac = [("vv", v_g_done), ("ss", s_h)]
        for ocg in range(8):
            bfv = P.dma("dg", pb[1][:, 512:1024],
                        bflat_in[:, ocg * 512:(ocg + 1) * 512], slot="bfslab")
            tl = None
            for dc in range(DC):
                sl = dc % 3
                wslot = (pb[0][:, 0:512], pb[0][:, 512:1024],
                         pb[1][:, 0:512])[sl]
                wv = P.dma("dg", wslot,
                           wt_w[dc * 128:(dc + 1) * 128,
                                ocg * 512:(ocg + 1) * 512], slot=f"wm{sl}",
                           waits=[("tt", t_gate)] if (ocg == 0 and dc < 3)
                           else ())
                for oc4 in range(4):
                    for tp in range(2):
                        wts = []
                        if oc4 == 0 and tp == 0:
                            wts.append(("dg", wv))
                            if dc == 0:
                                wts += evac
                        tl = P.op("tensor", lambda e, W=wslot, b_=oc4 * 2 + tp,
                                  o=oc4, c=dc * 1024 + tp * 512, f=(dc == 0):
                                  e.matmul(ps[b_][:, :],
                                           W[:, o * 128:(o + 1) * 128],
                                           S[:, c:c + 512], start=f,
                                           stop=False), waits=wts,
                                  sem="tt" if (oc4 == 3 and tp == 1) else None)
                P.setg(f"wm{sl}", "tt", tl)
            for oc4 in range(4):
                for tp in range(2):
                    wts = []
                    if oc4 == 0 and tp == 0:
                        wts.append(("dg", bfv))
                    tl = P.op("tensor", lambda e, b_=oc4 * 2 + tp, o=oc4,
                              c=tp * 512: e.matmul(
                                  ps[b_][:, :],
                                  pb[1][:, 512 + o * 128:512 + (o + 1) * 128],
                                  pb[3][:, c:c + 512], start=False, stop=True),
                              waits=wts,
                              sem="tt" if (oc4 == 3 and tp == 1) else None)
            t_main = P.cnt["tt"]
            P.setg("bfslab", "tt", t_main)
            for oc4 in range(4):
                for tp in range(2):
                    col = (ocg * 4 + oc4) * 1024 + tp * 512
                    P.op("vector", lambda e, b_=oc4 * 2 + tp, c=col:
                         e.scalar_tensor_tensor(
                             out=R[:, c:c + 512], in0=ps[b_][:, :], scalar=0.5,
                             in1=R[:, c:c + 512], op0=ALU.mult, op1=ALU.add),
                         sem="vv", waits=[("tt", t_main)])
            evac = [("vv", P.cnt["vv"])]
        v_evac = P.cnt["vv"]
        for oc in range(DC):
            P.op("scalar", lambda e, o=oc: e.add(
                out=R[:, o * 1024:(o + 1) * 1024],
                in_=R[:, o * 1024:(o + 1) * 1024], add=b_fm[:, o:o + 1]),
                sem="ss" if oc == DC - 1 else None,
                waits=[("vv", v_evac), ("ss", s_halfb)] if oc == 0 else ())

    def dump_R(waits):
        for dc in range(DC):
            P.dma("dh", out_t[dc * 128:(dc + 1) * 128, :],
                  R[:, dc * 1024:(dc + 1) * 1024],
                  waits=waits if dc == 0 else ())
        P.wait("sync", "dh", P.cnt["dh"])

    def dump_S(waits):
        for dc in range(DC):
            P.dma("dg", out_t[dc * 128:(dc + 1) * 128, :],
                  S[:, dc * 1024:(dc + 1) * 1024],
                  waits=waits if dc == 0 else ())
        P.wait("gpsimd", "dg", P.cnt["dg"])

    def dump_dram(t, waits):
        P.dma("dg", dap(out_t, 0, [[32768, 128], [1, 32768]]),
              dap(t, 0, [[32768, 128], [1, 32768]]), waits=waits)
        P.wait("gpsimd", "dg", P.cnt["dg"])

    def finish():
        with (nc.semaphore("s_dg") as s_dg, nc.semaphore("s_dh") as s_dh,
              nc.semaphore("s_tt") as s_tt, nc.semaphore("s_ss") as s_ss,
              nc.semaphore("s_vv") as s_vv, nc.semaphore("s_cc") as s_cc,
              nc.Block() as block):
            P.sems.update(dg=s_dg, dh=s_dh, tt=s_tt, ss=s_ss, vv=s_vv,
                          cc=s_cc)

            def runner(name):
                def run(e):
                    for eng, fn in P.ops:
                        if eng == name:
                            fn(e)
                return run
            block.gpsimd(runner("gpsimd"))
            block.sync(runner("sync"))
            block.tensor(runner("tensor"))
            block.scalar(runner("scalar"))
            block.vector(runner("vector"))
        return nc

    if tap == "xin":
        P.wait("sync", "dh", x_dh[-1])
        dump_R([])
        return finish()

    emit_ln("S", xwaits=x_dh)
    if tap == "n1":
        dump_S([("vv", P.cnt["vv"])])
        return finish()

    emit_ffn()
    if tap == "y1":
        dump_R([("ss", P.cnt["ss"]), ("vv", P.cnt["vv"])])
        return finish()

    emit_ln("pay")
    cc1 = P.collective(pay_n2.ap().opt(), gath_n2.ap().opt(),
                       waits=[("dh", P.cnt["dh"])])
    if tap == "n2g":
        dump_dram(gath_n2, [("cc", cc1)])
        return finish()

    P.barrier()
    P.seed("Sw", "qkr0", "qkr1", "qks0", "qks1")
    ln2_vv = P.cnt["vv"]
    qk_ps = [("vv", ln2_vv)]
    for (wsrc, dst_dram, bcol, nm) in ((wt_q, q_loc, bq_fm, "q"),
                                       (wt_k, k_loc, bk_fm, "k")):
        for dc in range(DC):
            P.dma("dg", S[:, dc * 1024:(dc + 1) * 1024],
                  wsrc[dc * 128:(dc + 1) * 128, :],
                  slot="Sw" if dc == 0 else None)
        wsl_dg = P.cnt["dg"]
        for tg in range(8):
            tl = None
            for dc in range(DC):
                pr = dc % 2
                rhs = pb[2][:, pr * 512:(pr + 1) * 512]
                off = (tg // 2) * RKB + dc * 131072 + (tg % 2) * 512
                rv = P.dma("dh", rhs,
                           dap(gath_n2, off, [[1024, 128], [1, 512]]),
                           slot=f"qkr{pr}",
                           waits=[("cc", cc1)] if (nm == "q" and tg == 0 and
                                                   dc < 2) else ())
                for oc in range(8):
                    wts = []
                    if oc == 0:
                        wts.append(("dh", rv))
                        if dc == 0:
                            wts.append(("dg", wsl_dg))
                            wts += qk_ps
                    tl = P.op("tensor", lambda e, b_=oc,
                              c=dc * 1024 + oc * 128, r=rhs, f=(dc == 0),
                              l=(dc == DC - 1): e.matmul(
                                  ps[b_][:, :], S[:, c:c + 128], r,
                                  start=f, stop=l), waits=wts,
                              sem="tt" if oc == 7 else None)
                P.setg(f"qkr{pr}", "tt", tl)
            t_tg = P.cnt["tt"]
            for oc in range(8):
                pr = oc % 2
                stg = pb[3][:, pr * 512:(pr + 1) * 512]
                sv = P.op("scalar", lambda e, b_=oc, g_=stg, o=oc, bc=bcol:
                          e.add(out=g_, in_=ps[b_][:, :], add=bc[:, o:o + 1]),
                          sem="ss", waits=[("tt", t_tg)], slot=f"qks{pr}")
                off = (oc * 128) * 4096 + tg * 512
                dv = P.dma("dh", dap(dst_dram, off, [[4096, 128], [1, 512]]),
                           stg, waits=[("ss", sv)])
                P.setg(f"qks{pr}", "dh", dv)
            qk_ps = [("ss", P.cnt["ss"])]
        P.setg("Sw", "tt", P.cnt["tt"])

    P.seed("vr0", "vr1", "vr2", "vs0", "vs1")
    for dc in range(DC):
        P.dma("dg", S[:, dc * 1024:(dc + 1) * 1024],
              wt_v[dc * 128:(dc + 1) * 128, :],
              slot="Sw" if dc == 0 else None)
    wsl_dg = P.cnt["dg"]
    v_ps = [qk_ps[0]]
    for Tc in range(32):
        r, tl0 = Tc // 8, (Tc % 8) * 128
        ring = {}
        for o8 in range(3):
            off = r * RKB + o8 * 8 * 131072 + tl0
            rv = P.dma("dh", pb[o8][:, :],
                       dap(gath_n2, off, [[1024, 128], [131072, 8], [1, 128]]),
                       slot=f"vr{o8}")
            ring[o8] = (pb[o8], rv)
        tl = None
        for dc in range(DC):
            o8, j = dc // 8, dc % 8
            if dc == 8:
                off = r * RKB + 3 * 8 * 131072 + tl0
                rv3 = P.dma("dh", pb[0][:, :],
                            dap(gath_n2, off,
                                [[1024, 128], [131072, 8], [1, 128]]),
                            slot="vr0")
                ring[3] = (pb[0], rv3)
            buf, rv = ring[o8]
            for oh in range(2):
                b_ = (Tc % 2) * 2 + oh + 4
                wts = []
                if oh == 0 and j == 0:
                    wts.append(("dh", rv))
                    if dc == 0:
                        wts.append(("dg", wsl_dg))
                        wts += v_ps
                tl = P.op("tensor", lambda e, b__=b_, m=buf, J=j,
                          c=dc * 1024 + oh * 512, f=(dc == 0),
                          l=(dc == DC - 1): e.matmul(
                              ps[b__][:, :], m[:, J * 128:(J + 1) * 128],
                              S[:, c:c + 512], start=f, stop=l),
                          waits=wts,
                          sem="tt" if (oh == 1 and (j == 7 or dc == DC - 1))
                          else None)
            if j == 7:
                P.setg("vr0" if o8 in (0, 3) else f"vr{o8}", "tt", tl)
        t_tc = P.cnt["tt"]
        for oh in range(2):
            b_ = (Tc % 2) * 2 + oh + 4
            stg = pb[3][:, oh * 512:(oh + 1) * 512]
            sv = P.op("scalar", lambda e, b__=b_, g_=stg: e.copy(
                out=g_, in_=ps[b__][:, :]), sem="ss", waits=[("tt", t_tc)],
                slot=f"vs{oh}")
            off = (Tc * 128) * 1024 + oh * 512
            dv = P.dma("dh", dap(v_loc, off, [[1024, 128], [1, 512]]), stg,
                       waits=[("ss", sv)])
            P.setg(f"vs{oh}", "dh", dv)
        v_ps = [("ss", P.cnt["ss"])]
    qkv_dh = P.cnt["dh"]
    P.setg("Sw", "tt", P.cnt["tt"])

    if tap in ("q", "k", "v"):
        src = {"q": q_loc, "k": k_loc, "v": v_loc}[tap]
        dump_dram(src, [("dh", qkv_dh)])
        return finish()

    P.barrier()
    P.seed("hd0", "hd1", "nrm_sq", "nrm_ps", "lbf", "ptd", "pt0", "pt1",
           "aos0", "aos1", "st_ps0", "st_ps1", "av_ps", "l_ps", "rb_ps",
           "statf_rb", "pf0t1", "stat_rows", "bc_col")
    for h in range(8):
        base = (h % 2) * 12288
        Kh = S[:, base:base + 4096]
        Vh = S[:, base + 4096:base + 8192]
        Qh = S[:, base + 8192:base + 12288]
        P.dma("dh", Kh, dap(k_loc, (h * 128) * 4096, [[4096, 128], [1, 4096]]),
              slot=f"hd{h % 2}",
              waits=[("dh", qkv_dh)] if h == 0 else ())
        for kq in range(8):
            P.dma("dh", Vh[:, kq * 512:(kq + 1) * 512],
                  dap(v_loc, kq * 4 * 131072 + h * 128,
                      [[1024, 128], [131072, 4], [1, 128]]))
        P.dma("dh", Qh, dap(q_loc, (h * 128) * 4096, [[4096, 128], [1, 4096]]))
        hd_dh = P.cnt["dh"]
        for idx, src in ((0, Kh), (1, Qh)):
            for tg in range(8):
                vq = P.op("vector", lambda e, s_=src, c=tg * 512:
                          e.tensor_mul(out=pb[1][:, 0:512], in0=s_[:, c:c + 512],
                                       in1=s_[:, c:c + 512]),
                          sem="vv", slot="nrm_sq",
                          waits=[("dh", hd_dh)] if (idx == 0 and tg == 0)
                          else ())
                tv = P.op("tensor", lambda e: e.matmul(
                    ps[5][0:1, :], ones128b[:, :], pb[1][:, 0:512],
                    start=True, stop=True), sem="tt", waits=[("vv", vq)],
                    slot="nrm_ps")
                P.setg("nrm_sq", "tt", tv)
                vr = P.op("vector", lambda e, c=idx * 8 + tg: e.reduce_max(
                    out=statf[0:1, c:c + 1], in_=ps[5][0:1, :],
                    axis=mybir.AxisListType.X), sem="vv", waits=[("tt", tv)],
                    slot="stat_rows")
                P.setg("nrm_ps", "vv", vr)
        v_nrm = P.op("vector", lambda e: e.reduce_max(
            out=statf[0:1, 16:17], in_=statf[0:1, 0:8],
            axis=mybir.AxisListType.X), sem="vv", waits=[("vv", P.cnt["vv"])])
        v_nrm = P.op("vector", lambda e: e.reduce_max(
            out=statf[0:1, 17:18], in_=statf[0:1, 8:16],
            axis=mybir.AxisListType.X), sem="vv", waits=[("vv", v_nrm)])
        v_nrm = P.op("vector", lambda e: e.tensor_mul(
            out=statf[0:1, 18:19], in0=statf[0:1, 16:17],
            in1=statf[0:1, 17:18]), sem="vv", waits=[("vv", v_nrm)])
        s_ln = P.op("scalar", lambda e: e.activation(
            out=statf[0:1, 19:20], in_=statf[0:1, 18:19], func=AF.Ln),
            sem="ss", waits=[("vv", v_nrm)])
        s_ex = P.op("scalar", lambda e: e.activation(
            out=statf[0:1, 20:21], in_=statf[0:1, 19:20], func=AF.Exp,
            scale=0.5), sem="ss", waits=[("ss", s_ln)])
        P.setg("stat_rows", "ss", s_ex)
        t_bc = P.op("tensor", lambda e: e.matmul(
            ps[6][:, 0:1], onesneg[0:1, 0:128], statf[0:1, 20:21],
            start=True, stop=True), sem="tt", waits=[("ss", s_ex)],
            slot="rb_ps")
        s_bc = P.op("scalar", lambda e: e.mul(out=biascol[:, :],
                                              in_=ps[6][:, 0:1], mul=-ISQ_HD),
                    sem="ss", waits=[("tt", t_bc)], slot="bc_col")
        P.setg("rb_ps", "ss", s_bc)
        if tap == "nrm" and h == 0:
            P.op("scalar", lambda e: e.copy(out=pf0[:, 0:1],
                                            in_=biascol[:, :]),
                 sem="ss", waits=[("ss", s_bc)])
            P.dma("dh", out_t[0:128, 0:4], pf0[:, 0:4],
                  waits=[("ss", P.cnt["ss"])])
            P.dma("dh", out_t[128:129, 0:512], statf[0:1, :])
            P.dma("dg", out_t[256:384, 0:1024], pb[1][:, :])
            P.wait("gpsimd", "dg", P.cnt["dg"])
            P.wait("sync", "dh", P.cnt["dh"])
            return finish()

        for g in range(8):
            nk = 4 * g + 4
            t_av = None
            for kc in range(nk):
                pr = kc % 2
                diag = kc >= 4 * g
                m = kc - 4 * g if diag else 0
                qlo = m * 128
                ncols = 512 - qlo
                t_st = P.op("tensor", lambda e, p_=pr, K=kc, Kh_=Kh, Qh_=Qh,
                            Q=g * 512 + qlo, n=ncols: e.matmul(
                                ps[p_][:, 0:n], Kh_[:, K * 128:(K + 1) * 128],
                                Qh_[:, Q:Q + n], start=True, stop=True),
                            sem="tt",
                            waits=[("ss", s_bc)] if kc == 0 else (),
                            slot=f"st_ps{pr}")
                PT = pb[2][:, pr * 512:(pr + 1) * 512]
                if diag:
                    s_e1 = P.op("scalar", lambda e, p_=pr: e.activation(
                        out=pb[0][:, 512:640], in_=ps[p_][:, 0:128],
                        func=AF.Exp, bias=biascol[:, :], scale=ISQ_HD),
                        sem="ss", waits=[("tt", t_st)], slot="ptd")
                    if ncols > 128:
                        P.op("scalar", lambda e, p_=pr, PT_=PT, q=qlo,
                             n=ncols - 128: e.activation(
                                 out=PT_[:, q + 128:512],
                                 in_=ps[p_][:, 128:128 + n], func=AF.Exp,
                                 bias=biascol[:, :], scale=ISQ_HD),
                             sem="ss", slot=f"pt{pr}")
                    s_pt = P.cnt["ss"]
                    v_mask = P.op("vector", lambda e, PT_=PT, q=qlo:
                                  e.tensor_mul(out=PT_[:, q:q + 128],
                                               in0=pb[0][:, 512:640],
                                               in1=trimask[:, :]),
                                  sem="vv", waits=[("ss", s_pt)],
                                  slot=f"pt{pr}")
                    P.setg("ptd", "vv", v_mask)
                    pt_ready = [("ss", s_pt), ("vv", v_mask)]
                else:
                    s_pt = P.op("scalar", lambda e, p_=pr, PT_=PT:
                                e.activation(out=PT_, in_=ps[p_][:, :],
                                             func=AF.Exp, bias=biascol[:, :],
                                             scale=ISQ_HD), sem="ss",
                                waits=[("tt", t_st)], slot=f"pt{pr}")
                    pt_ready = [("ss", s_pt)]
                P.setg(f"st_ps{pr}", *pt_ready[0])
                v_l = P.op("vector", lambda e, PT_=PT, q=qlo,
                           first=(kc == 0):
                           (e.tensor_copy(out=pf1[:, q:512], in_=PT_[:, q:512])
                            if first else
                            e.tensor_add(out=pf1[:, q:512], in0=pf1[:, q:512],
                                         in1=PT_[:, q:512])), sem="vv",
                           waits=pt_ready)
                wts = list(pt_ready)
                if kc == 0:
                    wts += P.guard["av_ps"]
                t_av = P.op("tensor", lambda e, K=kc, Vh_=Vh, PT_=PT, q=qlo,
                            f=(kc == 0), l=(kc == nk - 1): e.matmul(
                                ps[2][:, q:512],
                                Vh_[:, K * 128:(K + 1) * 128],
                                PT_[:, q:512], start=f, stop=l),
                            sem="tt", waits=wts)
                P.setg(f"pt{pr}", "tt", t_av)
            v_lb = P.op("vector", lambda e: e.tensor_copy(
                out=pb[0][:, 0:512], in_=pf1[:, :]), sem="vv", slot="lbf",
                waits=[("vv", P.cnt["vv"])])
            t_l = P.op("tensor", lambda e: e.matmul(
                ps[3][0:1, :], ones128b[:, :], pb[0][:, 0:512], start=True,
                stop=True), sem="tt", waits=[("vv", v_lb)], slot="l_ps")
            P.setg("lbf", "tt", t_l)
            v_rc = P.op("vector", lambda e: e.reciprocal(
                out=st_mean, in_=ps[3][0:1, :]), sem="vv", waits=[("tt", t_l)],
                slot="stat_rows")
            P.setg("l_ps", "vv", v_rc)
            t_rb = P.op("tensor", lambda e: e.matmul(
                ps[6][:, :], onesneg[0:1, 0:128], st_mean, start=True,
                stop=True), sem="tt", waits=[("vv", v_rc)], slot="rb_ps")
            P.setg("stat_rows", "tt", t_rb)
            s_rb = P.op("scalar", lambda e: e.copy(out=statf[:, :],
                                                   in_=ps[6][:, :]),
                        sem="ss", waits=[("tt", t_rb)], slot="statf_rb")
            P.setg("rb_ps", "ss", s_rb)
            v_t1 = P.op("vector", lambda e: e.tensor_mul(
                out=pf0[:, :], in0=ps[2][:, :], in1=statf[:, :]), sem="vv",
                waits=[("ss", s_rb), ("tt", t_av)], slot="pf0t1")
            P.setg("av_ps", "vv", v_t1)
            P.setg("statf_rb", "vv", v_t1)
            pr = g % 2
            stg = pb[3][:, pr * 512:(pr + 1) * 512]
            s_o = P.op("scalar", lambda e, g_=stg, H=h: e.add(
                out=g_, in_=pf0[:, :], add=bv_fm[:, H:H + 1]), sem="ss",
                waits=[("vv", v_t1)], slot=f"aos{pr}")
            P.setg("pf0t1", "ss", s_o)
            off = (h * 128) * 4096 + g * 512
            dv = P.dma("dh", dap(pay_ao, off, [[4096, 128], [1, 512]]), stg,
                       waits=[("ss", s_o)])
            P.setg(f"aos{pr}", "dh", dv)
        P.setg(f"hd{h % 2}", "tt", P.cnt["tt"])

    if tap == "ao_loc":
        dump_dram(pay_ao, [("dh", P.cnt["dh"])])
        return finish()
    cc2 = P.collective(pay_ao.ap().opt(), gath_ao.ap().opt(),
                       waits=[("dh", P.cnt["dh"])])
    if tap == "ao":
        dump_dram(gath_ao, [("cc", cc2)])
        return finish()

    P.barrier()
    P.seed("Sw", "wm0", "wm1", "wm2")
    def tokbase():
        pid = nc.partition_id([SP])
        return (pid % 4) * 1024

    P.dma_dyn("dh",
              (lambda: S[:, :]),
              (lambda: dap(gath_ao, tokbase(),
                           [[4096, 128], [RKB, 4], [524288, 8], [1, 1024]])),
              slot="Sw", waits=[("cc", cc2)])
    ao_dh = P.cnt["dh"]
    if tap == "aoS":
        dump_S([("dh", ao_dh)])
        return finish()

    evac = [("vv", P.cnt["vv"]), ("ss", P.cnt["ss"])]
    for ocg in range(8):
        tl = None
        for dc in range(DC):
            sl = dc % 3
            wslot = (pb[0][:, 0:512], pb[0][:, 512:1024], pb[1][:, 0:512])[sl]
            wv = P.dma("dg", wslot,
                       wt_wo[dc * 128:(dc + 1) * 128,
                             ocg * 512:(ocg + 1) * 512], slot=f"wm{sl}")
            for oc4 in range(4):
                for tp in range(2):
                    wts = []
                    if oc4 == 0 and tp == 0:
                        wts.append(("dg", wv))
                        if dc == 0:
                            wts += evac
                            if ocg == 0:
                                wts.append(("dh", ao_dh))
                    tl = P.op("tensor", lambda e, W=wslot, b_=oc4 * 2 + tp,
                              o=oc4, c=dc * 1024 + tp * 512, f=(dc == 0),
                              l=(dc == DC - 1): e.matmul(
                                  ps[b_][:, :], W[:, o * 128:(o + 1) * 128],
                                  S[:, c:c + 512], start=f, stop=l),
                              waits=wts,
                              sem="tt" if (oc4 == 3 and tp == 1) else None)
            P.setg(f"wm{sl}", "tt", tl)
        t_main = P.cnt["tt"]
        for oc4 in range(4):
            for tp in range(2):
                col = (ocg * 4 + oc4) * 1024 + tp * 512
                P.op("vector", lambda e, b_=oc4 * 2 + tp, c=col:
                     e.scalar_tensor_tensor(
                         out=R[:, c:c + 512], in0=ps[b_][:, :], scalar=1.0,
                         in1=R[:, c:c + 512], op0=ALU.mult, op1=ALU.add),
                     sem="vv", waits=[("tt", t_main)])
        evac = [("vv", P.cnt["vv"])]
    v_evac = P.cnt["vv"]
    for oc in range(DC):
        P.op("scalar", lambda e, o=oc: e.add(
            out=R[:, o * 1024:(o + 1) * 1024],
            in_=R[:, o * 1024:(o + 1) * 1024], add=bo_fm[:, o:o + 1]),
            sem="ss" if oc == DC - 1 else None,
            waits=[("vv", v_evac)] if oc == 0 else ())
    if tap == "y2":
        dump_R([("ss", P.cnt["ss"]), ("vv", P.cnt["vv"])])
        return finish()

    emit_ln("S")
    emit_ffn()

    fin_w = [("ss", P.cnt["ss"]), ("vv", P.cnt["vv"])]
    for dc in range(DC):
        P.dma("dh", out_t[dc * 128:(dc + 1) * 128, :],
              R[:, dc * 1024:(dc + 1) * 1024],
              waits=fin_w if dc == 0 else ())
    fin = P.cnt["dh"]
    P.wait("sync", "dh", fin)
    P.wait("gpsimd", "dh", fin)
    return finish()


_NC_CACHE = {}


def _get_nc(tap=None):
    if tap not in _NC_CACHE:
        _NC_CACHE[tap] = build_program(tap)
    return _NC_CACHE[tap]


def _prep_inputs(inputs):
    f32 = np.float32
    x = np.asarray(inputs["L0_attn_x"], f32)
    Wq = np.asarray(inputs["L0_attn_Wq"], f32)
    bq = np.asarray(inputs["L0_attn_bq"], f32)
    Wk = np.asarray(inputs["L0_attn_Wk"], f32)
    bk = np.asarray(inputs["L0_attn_bk"], f32)
    Wv = np.asarray(inputs["L0_attn_Wv"], f32)
    bv = np.asarray(inputs["L0_attn_bv"], f32)
    Wo = np.asarray(inputs["L0_attn_Wo"], f32)
    bo = np.asarray(inputs["L0_attn_bo"], f32)
    W = np.asarray(inputs["L0_ffn_W"], f32)
    b = np.asarray(inputs["L0_ffn_b"], f32)
    mWq = np.asarray(inputs["L0_ffn_Wq"], f32)
    mWk = np.asarray(inputs["L0_ffn_Wk"], f32)
    Am = np.asarray(inputs["L0_ffn_A"], f32)
    Bm = np.asarray(inputs["L0_ffn_Bm"], f32)

    sh = {
        "wt_w": np.ascontiguousarray(W.T),
        "wt_wo": np.ascontiguousarray(Wo.T),
        "wt_mq": np.ascontiguousarray(mWq.T),
        "wt_mk": np.ascontiguousarray(mWk.T),
        "wt_a": np.ascontiguousarray(Am.reshape(128, D).T),
        "bflat": np.ascontiguousarray(Bm.transpose(0, 2, 1).reshape(128, D)),
        "b_fm": np.ascontiguousarray(b.reshape(32, 128).T),
        "bo_fm": np.ascontiguousarray(bo.reshape(32, 128).T),
        "selq": np.repeat(np.eye(8, dtype=f32), 32, axis=0)
                * f32(1.0 / np.sqrt(32.0)),
        "e16": np.repeat(np.eye(8, dtype=f32), 16, axis=1).reshape(8, 128),
        "trimask": np.triu(np.ones((128, 128), dtype=f32)),
    }
    WqT = np.ascontiguousarray(Wq.T)
    WkT = np.ascontiguousarray(Wk.T)
    WvT = np.ascontiguousarray(Wv.T)
    in_maps = []
    for c in range(8):
        g, i = c // 4, c % 4
        sl = slice(1024 * i, 1024 * (i + 1))
        m = dict(sh)
        m["xt"] = np.ascontiguousarray(x[g, sl, :].T)
        m["wt_q"] = np.ascontiguousarray(WqT[:, sl])
        m["wt_k"] = np.ascontiguousarray(WkT[:, sl])
        m["wt_v"] = np.ascontiguousarray(WvT[:, sl])
        m["bq_fm"] = np.ascontiguousarray(bq[sl].reshape(8, 128).T)
        m["bk_fm"] = np.ascontiguousarray(bk[sl].reshape(8, 128).T)
        m["bv_fm"] = np.ascontiguousarray(bv[sl].reshape(8, 128).T)
        in_maps.append(m)
    return in_maps


def run(inputs, tap=None, trace=False):
    nc = _get_nc(tap)
    in_maps = _prep_inputs(inputs)
    return run_bass_kernel_spmd(nc, in_maps, core_ids=list(range(8)),
                                trace=trace)


def kernel(**inputs):
    res = run(inputs)
    out = np.empty((2, 4096, 4096), dtype=np.float32)
    for c in range(8):
        g, i = c // 4, c % 4
        out[g, 1024 * i:1024 * (i + 1), :] = res.results[c]["out_t"].T
    return out
